# version 02 (final)
# baseline (speedup 1.0000x reference)
import numpy as np
import concourse.bass as bass
import concourse.mybir as mybir
from concourse.bass_utils import run_bass_kernel_spmd
from contextlib import ExitStack

F32 = mybir.dt.float32
BF16 = mybir.dt.bfloat16
I32 = mybir.dt.int32
ALU = mybir.AluOpType
AF = mybir.ActivationFunctionType
AX = mybir.AxisListType

P = 128
C = 1024
CH = 512
NH = 8
N = 64
T = 8192
L = 64
SEG = 256
NSEG = T // SEG
CPS = SEG // L
NCK = C // P
SCK = CH // P
OT = 4096
FI = 4096
NEX = NSEG // 2
OSEG = OT // SEG
CDEC = float(np.exp(-0.5))
EPS_LN = 1e-5
EPS_GN = 0.00064
NLVL = 5
CW = SCK * N


def ts(i, sz):
    return slice(i * sz, (i + 1) * sz)


def _os_dbg():
    import os
    return os.environ.get("DBG_SEG", "") == "1"


def build_nc():
    nc = bass.Bass("TRN2", target_bir_lowering=False, debug=False,
                   detect_race_conditions=False)

    din = lambda name, shape, dt=F32: nc.dram_tensor(name, shape, dt, kind="ExternalInput").ap()
    d_x = din("x_seq", [T, C])
    d_xres = din("x_res", [OT, C])
    d_segb = din("segbase", [1, 1], I32)
    d_wr = din("wr", [C, CH])
    d_wk = din("wk", [C, CH])
    d_wv = din("wv", [C, CH])
    d_wo = din("wo", [C, C])
    d_l1 = din("l1", [C, 256])
    d_w2a2 = din("w2a2", [N, 2 * CH])
    d_g2 = din("g2", [P, CH])
    d_mixv = din("mixv", [P, 6 * NCK])
    d_chc = din("chconst", [P, 8 * SCK])
    d_wg = din("wgate", [FI, C])
    d_wu = din("wup", [FI, C])
    d_wd = din("wdown", [C, FI])
    d_ident = din("identc", [P, P], BF16)
    d_ones1 = din("ones1", [P, P], BF16)
    d_onesS = din("onesS", [P, P], BF16)
    d_blockS = din("blockS", [P, P], BF16)
    d_blockU = din("blockU", [P, P], BF16)
    d_mcomb = din("m_comb", [P, P], BF16)
    d_mls = din("m_ls64", [P, N], BF16)
    d_i64 = din("i64", [P, N], BF16)
    d_selm = din("selm", [P, 512], BF16)

    d_out = nc.dram_tensor("out_fm", [C, OT], F32, kind="ExternalOutput").ap()
    d_dbg = nc.dram_tensor("dbg", [P, 49664], F32, kind="ExternalOutput").ap()
    DBG_ON = _os_dbg()

    d_qwg = nc.dram_tensor("qwg", [C, FI], BF16, kind="Internal").ap()
    d_qwu = nc.dram_tensor("qwu", [C, FI], BF16, kind="Internal").ap()
    d_qwd = nc.dram_tensor("qwd", [FI, C], BF16, kind="Internal").ap()
    d_x1 = nc.dram_tensor("x1d", [C, OT], BF16, kind="Internal").ap()
    d_hbuf = nc.dram_tensor("hbuf", [FI, OT], BF16, kind="Internal").ap()
    d_ccin = nc.dram_tensor("ccin", [NEX, P, SCK * 2 * SEG], BF16, kind="Internal").ap()
    d_ccout = nc.dram_tensor("ccout", [NEX, 2, P, SCK * 2 * SEG], BF16, kind="Internal").ap()
    d_ccmine = nc.dram_tensor("ccmine", [NEX // 2, 2, P, SCK * 2 * SEG], BF16, kind="Internal").ap()
    groups = [[0, 1], [2, 3], [4, 5], [6, 7]]

    es = ExitStack()
    sbuf = lambda name, shape, dt=BF16: es.enter_context(nc.sbuf_tensor(name, shape, dt))
    psum = lambda name, shape, dt=F32: es.enter_context(nc.psum_tensor(name, shape, dt))

    identS = sbuf("identS", [P, P])
    ones1S = sbuf("ones1S", [P, P])
    onesSS = sbuf("onesSS", [P, P])
    blockSS = sbuf("blockSS", [P, P])
    blockUS = sbuf("blockUS", [P, P])
    mcombS = sbuf("mcombS", [P, P])
    mlsS = sbuf("mlsS", [P, N])
    i64S = sbuf("i64S", [P, N])
    mixvS = sbuf("mixvS", [P, 6 * NCK], F32)
    chcS = sbuf("chcS", [P, 8 * SCK], F32)
    segbS = sbuf("segbS", [1, 1], I32)
    ones1F = sbuf("ones1F", [1, P], F32)
    ones1B = sbuf("ones1B", [1, P])
    cEpsLN = sbuf("cEpsLN", [P, 1], F32)
    cEpsGN = sbuf("cEpsGN", [P, 1], F32)
    stF = sbuf("stF", [P, SCK * N], F32)
    stB = sbuf("stB", [P, SCK * N])
    wacc = sbuf("wacc", [P, 8], F32)
    waccb = sbuf("waccb", [P, 128])
    waccb2 = sbuf("waccb2", [P, 128])
    wsc = sbuf("wsc", [P, 128], F32)
    wneg = sbuf("wneg", [P, 128], F32)
    winv = sbuf("winv", [P, 128], F32)
    wtmp = sbuf("wtmp", [P, 128], F32)
    waccf = sbuf("waccf", [P, 4], F32)
    fsc = sbuf("fsc", [P, 8], F32)

    pA = psum("pA", [P, 512])
    pB = psum("pB", [P, 512])
    pC = psum("pC", [P, 512])
    pD = psum("pD", [P, 512])
    pE = psum("pE", [P, 512])
    pF = psum("pF", [P, 512])
    pG = psum("pG", [P, 256])
    pT = psum("pT", [P, 1024], BF16)

    sems = {}
    for e in "tvagqw":
        sems[e] = es.enter_context(nc.semaphore("sem_" + e))

    class Arena:
        def __init__(self, handle, cols):
            self.h = handle
            self.cols = cols
            self.off = 0
        def take(self, n):
            n = (n + 3) & ~3
            assert self.off + n <= self.cols, f"arena overflow {self.off}+{n}>{self.cols}"
            ap = self.h[:, self.off:self.off + n]
            self.off += n
            return ap
        def mark(self):
            return self.off
        def reset(self, m):
            self.off = m

    AB = Arena(es.enter_context(nc.sbuf_tensor("arenaB", [P, 77568], BF16)), 77568)
    ARF = Arena(es.enter_context(nc.sbuf_tensor("arenaF", [P, 10500], F32)), 10500)
    AI = Arena(es.enter_context(nc.sbuf_tensor("arenaI", [P, 2048], I32)), 2048)

    DBG = {}
    nc._dbg_map = DBG

    def atake(shape, dt=BF16, name=None):
        assert shape[0] == P
        if dt is BF16:
            ap = AB.take(shape[1]); DBG[name] = ("arenaB", ap)
        elif dt is F32:
            ap = ARF.take(shape[1]); DBG[name] = ("arenaF", ap)
        else:
            ap = AI.take(shape[1]); DBG[name] = ("arenaI", ap)
        return ap

    PLAN = {k: [] for k in "tvagq"}
    CNT = {k: 0 for k in "tvagqw"}
    import os as _os
    STEP_LIMIT = int(_os.environ.get("STEP_LIMIT", "1000000000"))
    NSTEPS = [0]

    def step(q_, sem_, incs, deps=()):
        def deco(fn):
            if NSTEPS[0] >= STEP_LIMIT:
                return (sem_, CNT[sem_])
            NSTEPS[0] += 1
            d = {}
            for de, dc in deps:
                if de != sem_ and dc > 0:
                    d[de] = max(d.get(de, 0), dc)
            PLAN[q_].append((d, fn, sem_, incs))
            CNT[sem_] += incs
            return (sem_, CNT[sem_])
        return deco

    def hsl(buf, h, w):
        pr = (h % 2) * N
        return buf[pr:pr + N, (h // 2) * w:(h // 2 + 1) * w]

    _m0B, _m0F, _m0I = AB.mark(), ARF.mark(), AI.mark()
    sb2 = lambda name, shape, dt=BF16: atake(shape, dt, name)

    wrS = sb2("wrS", [P, NCK * CH])
    wkS = sb2("wkS", [P, NCK * CH])
    wvS = sb2("wvS", [P, NCK * CH])
    l1S = sb2("l1S", [P, 2 * NCK * 256])
    w2a2S = sb2("w2a2S", [P, 2 * CH])
    hidA = sb2("hidA", [P, SEG])
    g2S = sb2("g2S", [P, CH])
    l1raw = sb2("l1raw", [P, NCK * 256], F32)
    wstage = sb2("wstage", [P, 1024], F32)
    wstage2 = sb2("wstage2", [P, 1024], F32)
    qtmp1 = sb2("qtmp1", [P, 1024], F32)
    qtmp2 = sb2("qtmp2", [P, 1024])
    qtmp3 = sb2("qtmp3", [P, 1024])

    xstage = sb2("xstage", [P, 2 * C])
    xfm = sb2("xfm", [P, NCK * SEG])
    xsq = sb2("xsq", [P, NCK * SEG])
    stat1 = sb2("stat1", [P, SEG], F32)
    stat2 = sb2("stat2", [P, SEG], F32)
    stat3 = sb2("stat3", [P, SEG], F32)
    stat2b = sb2("stat2b", [P, SEG])
    stat3b = sb2("stat3b", [P, SEG])
    xn = sb2("xn", [P, NCK * SEG])
    xh = sb2("xh", [P, NCK])
    xxb = sb2("xxb", [P, NCK * SEG])
    xr_m = sb2("xr_m", [P, NCK * SEG])
    xk_m = sb2("xk_m", [P, NCK * SEG])
    xv_m = sb2("xv_m", [P, NCK * SEG])
    hidWA = sb2("hidWA", [P, SEG])
    hidG = sb2("hidG", [P, SEG])
    sigw = sb2("sigw", [P, SCK * SEG], F32)
    csb = sb2("csb", [P, SCK * SEG], F32)
    r_s = sb2("r_s", [P, SCK * SEG])
    keff = sb2("keff", [P, SCK * SEG])
    kkn = sb2("kkn", [P, SCK * SEG])
    kkf = sb2("kkf", [P, SCK * SEG])
    kksq = sb2("kksq", [P, SCK * SEG])
    agt = sb2("agt", [P, SCK * SEG])
    vfm = sb2("vfm", [P, SCK * SEG])
    gfm = sb2("gfm", [P, SCK * SEG])
    m1t = sb2("m1t", [P, SCK * SEG])
    rsn = sb2("rsn", [P, SCK * SEG], F32)
    oseg = sb2("oseg", [P, SCK * SEG])
    o2sq = sb2("o2sq", [P, SCK * SEG])
    gnm = sb2("gnm", [P, SCK * SEG])
    gnr = sb2("gnr", [P, SCK * SEG])
    rowb = sb2("rowb", [P, SCK * SEG])
    yseg = sb2("yseg", [P, SCK * SEG])
    ygseg = sb2("ygseg", [P, SCK * SEG])

    Et = sb2("Et", [P, CW], F32)
    Einv = sb2("Einv", [P, CW], F32)
    Eiw = sb2("Eiw", [P, CW], F32)
    tmpE = sb2("tmpE", [P, CW], F32)
    tmpE2 = sb2("tmpE2", [P, CW], F32)
    ARh = sb2("ARh", [P, SCK * 2 * N])
    Bh_t = sb2("Bh_t", [P, CW])
    Kh_t = sb2("Kh_t", [P, CW])
    Vtok = sb2("Vtok", [P, CW])
    Btok = sb2("Btok", [P, CW])
    Ktok = sb2("Ktok", [P, CW])
    ipNR = sb2("ipNR", [P, SCK * 2 * N])
    ipMR = sb2("ipMR", [P, SCK * 2 * N])
    ipNt = sb2("ipNt", [P, SCK * N])
    neuA = sb2("neuA", [P, SCK * N])
    neuB = sb2("neuB", [P, SCK * N])
    neuA2 = sb2("neuA2", [P, SCK * N])
    neuB2 = sb2("neuB2", [P, SCK * N])
    qT = sb2("qT", [P, SCK * N])
    rhsS = sb2("rhsS", [P, SCK * N])
    uS = sb2("uS", [P, SCK * N])
    eclT = sb2("eclT", [P, SCK], F32)
    eclB = sb2("eclB", [P, SCK * N], F32)

    chccol = lambda v, pc: chcS[:, v * SCK + pc: v * SCK + pc + 1]

    def chview(buf, l, w=L):
        return buf[:].rearrange("p (pc t) -> p pc t", pc=SCK)[:, :, l * w:(l + 1) * w]

    @step("q", "q", 16 * 11)
    def st_loads(eng, sem):
        for dst, src in (
            (identS, d_ident), (ones1S, d_ones1), (onesSS, d_onesS),
            (blockSS, d_blockS), (blockUS, d_blockU), (mcombS, d_mcomb),
            (mlsS, d_mls), (i64S, d_i64), (mixvS, d_mixv), (chcS, d_chc),
        ):
            eng.dma_start(out=dst[:], in_=src).then_inc(sem, 16)
        return eng.dma_start(out=segbS[:], in_=d_segb).then_inc(sem, 16)

    @step("g", "w", 16 * 6)
    def st_cast(eng, sem):
        eng.dma_start(out=wrS[:].rearrange("p (kc o) -> p kc o", kc=NCK), in_=d_wr.rearrange("(kc p) o -> p kc o", p=P)).then_inc(sem, 16)
        eng.dma_start(out=wkS[:].rearrange("p (kc o) -> p kc o", kc=NCK), in_=d_wk.rearrange("(kc p) o -> p kc o", p=P)).then_inc(sem, 16)
        eng.dma_start(out=wvS[:].rearrange("p (kc o) -> p kc o", kc=NCK), in_=d_wv.rearrange("(kc p) o -> p kc o", p=P)).then_inc(sem, 16)
        eng.dma_start(out=w2a2S[0:N, :], in_=d_w2a2).then_inc(sem, 16)
        eng.dma_start(out=g2S[:], in_=d_g2).then_inc(sem, 16)
        return eng.dma_start(out=l1raw[:].rearrange("p (kc o) -> p kc o", kc=NCK), in_=d_l1.rearrange("(kc p) o -> p kc o", p=P)).then_inc(sem, 16)

    @step("v", "v", 1, deps=[st_loads, st_cast])
    def st_prep(eng, sem):
        eng.memset(stF[:], 0.0)
        eng.memset(stB[:], 0.0)
        eng.memset(xh[:], 0.0)
        eng.memset(ones1F[:], 1.0)
        eng.memset(ones1B[:], 1.0)
        eng.memset(cEpsLN[:], EPS_LN)
        eng.memset(cEpsGN[:], EPS_GN)
        eng.tensor_scalar(chcS[:, ts(7, SCK)], chcS[:, ts(1, SCK)], -1.0, 1.0, ALU.mult, ALU.add)
        eng.tensor_copy(l1S[:, 0:NCK * 256], l1raw[:])
        last = None
        for kc in range(NCK):
            for c0, c1, mi in ((0, 64, 1), (64, 128, 4), (128, 256, 5)):
                mcol = mixvS[:, mi * NCK + kc: mi * NCK + kc + 1]
                last = eng.tensor_scalar_mul(
                    l1S[:, (NCK + kc) * 256 + c0: (NCK + kc) * 256 + c1],
                    l1raw[:, kc * 256 + c0: kc * 256 + c1], mcol)
        return last.then_inc(sem, 1)

    mats = [(d_wg, d_qwg, 0), (d_wu, d_qwu, 1), (d_wd, d_qwd, 2)]

    def wsrc(d_raw, m, ch):
        if m < 2:
            return d_raw[ts(ch, P), :]
        return d_raw[ts(ch // 4, P), ts(ch % 4, 1024)]

    prev_v = st_prep
    for d_raw, d_q, m in mats:
        for ch in range(32):
            stg = wstage if ch % 2 == 0 else wstage2
            def ld(eng, sem, src=wsrc(d_raw, m, ch), stg=stg):
                return eng.dma_start(out=stg[:], in_=src).then_inc(sem, 16)
            stq = step("q", "q", 16, deps=[prev_v])(ld)
            def red(eng, sem, m=m, ch=ch, stg=stg):
                return eng.reduce_sum(l1raw[:, m * 32 + ch:m * 32 + ch + 1], stg[:],
                                      axis=AX.X, apply_absolute_value=True).then_inc(sem, 1)
            prev_v = step("v", "v", 1, deps=[stq])(red)

    @step("v", "v", 1, deps=[prev_v])
    def st_tree_v(eng, sem):
        fill = lambda: eng.tensor_copy(qtmp2[:], qtmp3[:])
        eng.memset(wtmp[:], 0.0)
        fill()
        eng.reduce_sum(wtmp[:, 0:3], l1raw[:, 0:96].rearrange("p (m c) -> p m c", m=3),
                       axis=AX.X, apply_absolute_value=False)
        fill()
        eng.tensor_copy(waccb[:], wtmp[:])
        fill()
        eng.tensor_copy(wneg[:], waccb[:])
        fill()
        eng.tensor_sub(wtmp[:], wtmp[:], wneg[:])
        fill()
        return eng.tensor_copy(waccb2[:], wtmp[:]).then_inc(sem, 1)

    @step("t", "t", 1, deps=[st_tree_v, st_loads])
    def st_tree_t(eng, sem):
        eng.matmul(pG[:, 0:128], ones1S[:], waccb[:], start=True, stop=False)
        return eng.matmul(pG[:, 0:128], ones1S[:], waccb2[:], start=False, stop=True).then_inc(sem, 1)

    @step("v", "v", 1, deps=[st_tree_t])
    def st_neg(eng, sem):
        fill = lambda: eng.tensor_copy(qtmp2[:], qtmp3[:])
        eng.tensor_copy(wsc[:], pG[:, 0:128])
        fill()
        eng.tensor_scalar_max(wtmp[:], wsc[:], float(FI * C * 1e-5))
        fill()
        eng.reciprocal(winv[:], wtmp[:])
        fill()
        eng.tensor_scalar_mul(wsc[:], winv[:], float(FI * C))
        fill()
        eng.tensor_scalar_mul(wneg[:], wsc[:], -1.0)
        fill()
        return eng.reciprocal(winv[:], wsc[:]).then_inc(sem, 1)

    prev_store = st_neg
    prev_tq = None
    for d_raw, d_q, m in mats:
        for ch in range(32):
            stg = wstage if ch % 2 == 0 else wstage2
            def ld(eng, sem, src=wsrc(d_raw, m, ch), stg=stg):
                return eng.dma_start(out=stg[:], in_=src).then_inc(sem, 16)
            stq = step("q", "q", 16, deps=[prev_store])(ld)
            def quant(eng, sem, m=m, stg=stg):
                eng.tensor_scalar(qtmp1[:], stg[:], wsc[:, m:m + 1], 0.5, ALU.mult, ALU.is_ge)
                eng.tensor_scalar(qtmp2[:], stg[:], wneg[:, m:m + 1], 0.5, ALU.mult, ALU.is_ge)
                return eng.tensor_sub(qtmp2[:], qtmp1[:], qtmp2[:]).then_inc(sem, 1)
            stv = step("v", "v", 1, deps=[stq, prev_tq] if prev_tq else [stq])(quant)
            def trans(eng, sem):
                last = None
                for b in range(8):
                    last = eng.transpose(pT[:, ts(b, P)], qtmp2[:, ts(b, P)], identS[:])
                return last.then_inc(sem, 1)
            stt_ = step("t", "t", 1, deps=[stv, prev_store])(trans)
            def evac(eng, sem):
                return eng.tensor_copy(qtmp3[:], pT[:]).then_inc(sem, 1)
            sev = step("v", "v", 1, deps=[stt_])(evac)
            prev_tq = sev
            if m < 2:
                dst = d_q.rearrange("(b p) f -> p b f", p=P)[:, :, ts(ch, P)]
            else:
                dst = d_qwd.rearrange("(b p) f -> p b f", p=P)[:, 8 * (ch % 4):8 * (ch % 4) + 8, ts(ch // 4, P)]
            def store(eng, sem, dst=dst):
                return eng.dma_start(out=dst, in_=qtmp3[:].rearrange("p (b f) -> p b f", b=8)).then_inc(sem, 16)
            prev_store = step("q", "q", 16, deps=[sev])(store)

    st_seg_done = st_prep
    st_scan_done = st_prep
    st_x_used = st_prep
    st_prev_ccst = st_prep
    st_cc = []
    prev_tr_t = prev_store

    for s in range(NSEG):
        def ldx(eng, sem, s=s):
            return eng.dma_start(
                out=xstage[:].rearrange("p (tt c) -> p tt c", tt=2),
                in_=d_x[ts(s, SEG), :].rearrange("(tt p) c -> p tt c", p=P)).then_inc(sem, 16)
        stA = step("g", "w", 16, deps=[st_x_used])(ldx)

        def trB1(eng, sem):
            last = None
            for kc in range(NCK):
                last = eng.transpose(pT[:, ts(kc, P)], xstage[:, kc * P:(kc + 1) * P], identS[:])
            return last.then_inc(sem, 1)
        def evB1(eng, sem):
            return eng.tensor_copy(
                xfm[:].rearrange("p (kc t) -> p kc t", kc=NCK)[:, :, 0:P],
                pT[:].rearrange("p (kc t) -> p kc t", kc=NCK)).then_inc(sem, 1)
        def trB2(eng, sem):
            last = None
            for kc in range(NCK):
                last = eng.transpose(pT[:, ts(kc, P)], xstage[:, C + kc * P: C + (kc + 1) * P], identS[:])
            return last.then_inc(sem, 1)
        def evB2(eng, sem):
            return eng.tensor_copy(
                xfm[:].rearrange("p (kc t) -> p kc t", kc=NCK)[:, :, P:SEG],
                pT[:].rearrange("p (kc t) -> p kc t", kc=NCK)).then_inc(sem, 1)
        stB1 = step("t", "t", 1, deps=[stA, prev_tr_t, st_seg_done])(trB1)
        stE1 = step("v", "v", 1, deps=[stB1])(evB1)
        stB2 = step("t", "t", 1, deps=[stA, stE1])(trB2)
        stE2 = step("v", "v", 1, deps=[stB2])(evB2)
        prev_tr_t = stB2

        def sq(eng, sem):
            return eng.activation(xsq[:], xfm[:], AF.Square).then_inc(sem, 1)
        stSq = step("a", "a", 1, deps=[stE2])(sq)

        def lnmm(eng, sem):
            for kc in range(NCK):
                eng.matmul(pA[:, 0:SEG], onesSS[:], xfm[:, ts(kc, SEG)], start=(kc == 0), stop=(kc == NCK - 1))
            last = None
            for kc in range(NCK):
                last = eng.matmul(pB[:, 0:SEG], onesSS[:], xsq[:, ts(kc, SEG)], start=(kc == 0), stop=(kc == NCK - 1))
            return last.then_inc(sem, 1)
        stMM = step("t", "t", 1, deps=[stSq, stE2, st_seg_done])(lnmm)

        def lnchain1(eng, sem):
            eng.tensor_copy(stat1[:], pA[:, 0:SEG])
            eng.tensor_mul(stat2[:], stat1[:], stat1[:])
            return eng.tensor_sub(stat2[:], pB[:, 0:SEG], stat2[:]).then_inc(sem, 1)
        stL1 = step("v", "v", 1, deps=[stMM])(lnchain1)

        def lnsqrt(eng, sem):
            return eng.activation(stat2[:], stat2[:], AF.Sqrt, bias=cEpsLN[:]).then_inc(sem, 1)
        stL2 = step("a", "a", 1, deps=[stL1])(lnsqrt)

        def lnchain2(eng, sem):
            eng.reciprocal(stat2[:], stat2[:])
            eng.tensor_mul(stat3[:], stat1[:], stat2[:])
            eng.tensor_copy(stat2b[:], stat2[:])
            return eng.tensor_copy(stat3b[:], stat3[:]).then_inc(sem, 1)
        stL3 = step("v", "v", 1, deps=[stL2])(lnchain2)

        def mixes(eng, sem):
            x3 = xfm[:].rearrange("p (kc t) -> p kc t", kc=NCK)
            n3 = xn[:].rearrange("p (kc t) -> p kc t", kc=NCK)
            rb = stat2b[:].unsqueeze(1).broadcast_to([P, NCK, SEG])
            mb = stat3b[:].unsqueeze(1).broadcast_to([P, NCK, SEG])
            eng.tensor_tensor(n3, x3, rb, ALU.mult)
            eng.tensor_tensor(n3, n3, mb, ALU.subtract)
            eng.tensor_sub(xxb[:, 1:NCK * SEG], xn[:, 0:NCK * SEG - 1], xn[:, 1:NCK * SEG])
            eng.tensor_tensor(xxb[:, 0::SEG], xh[:], xn[:, 0::SEG], ALU.subtract)
            eng.tensor_copy(xh[:], xn[:, SEG - 1::SEG])
            last = None
            for kc in range(NCK):
                for mi, dst in ((0, xr_m), (2, xk_m), (3, xv_m)):
                    mcol = mixvS[:, mi * NCK + kc: mi * NCK + kc + 1]
                    last = eng.scalar_tensor_tensor(
                        dst[:, ts(kc, SEG)], xxb[:, ts(kc, SEG)], mcol, xn[:, ts(kc, SEG)],
                        ALU.mult, ALU.add)
            return last.then_inc(sem, 1)
        stMix = step("v", "v", 1, deps=[stL3])(mixes)

        def loramm(eng, sem):
            last = None
            for oc2 in range(2):
                for kc in range(2 * NCK):
                    rhs = xn[:, ts(kc, SEG)] if kc < NCK else xxb[:, ts(kc - NCK, SEG)]
                    last = eng.matmul(pC[:, ts(oc2, SEG)],
                                      l1S[:, kc * 256 + oc2 * P: kc * 256 + (oc2 + 1) * P],
                                      rhs, start=(kc == 0), stop=(kc == 2 * NCK - 1))
            return last.then_inc(sem, 1)
        stLm = step("t", "t", 1, deps=[stMix])(loramm)

        def loraev(eng, sem):
            eng.activation(hidWA[0:N, :], pC[0:N, 0:SEG], AF.Tanh)
            eng.activation(hidA[0:N, :], pC[N:P, 0:SEG], AF.Identity)
            return eng.activation(hidG[:], pC[:, SEG:2 * SEG], AF.Sigmoid).then_inc(sem, 1)
        stLe = step("a", "a", 1, deps=[stLm])(loraev)

        st_prev = stLe
        for oc in range(SCK):
            def l2mm(eng, sem, oc=oc):
                _n = int(_os.environ.get("L2MM_N", "5"))
                eng.matmul(pC[0:N, 0:SEG], w2a2S[0:N, oc * P:oc * P + N], hidWA[0:N, :], start=True, stop=True)
                eng.matmul(pC[N:P, 0:SEG], w2a2S[0:N, oc * P + N:oc * P + P], hidWA[0:N, :], start=True, stop=True)
                eng.matmul(pC[0:N, SEG:SEG + SEG], w2a2S[0:N, CH + oc * P:CH + oc * P + N], hidA[0:N, :], start=True, stop=True)
                eng.matmul(pC[N:P, SEG:SEG + SEG], w2a2S[0:N, CH + oc * P + N:CH + oc * P + P], hidA[0:N, :], start=True, stop=True)
                return eng.matmul(pD[:, 0:SEG], g2S[:, ts(oc, P)], hidG[:], start=True, stop=True).then_inc(sem, 1)
            stm = step("t", "t", 1, deps=[st_prev])(l2mm)
            def l2ev(eng, sem, oc=oc):
                eng.activation(sigw[:, ts(oc, SEG)], pC[:, 0:SEG], AF.Sigmoid, bias=chccol(2, oc))
                eng.activation(agt[:, ts(oc, SEG)], pC[:, SEG:2 * SEG], AF.Sigmoid, bias=chccol(3, oc))
                return eng.activation(gfm[:, ts(oc, SEG)], pD[:, 0:SEG], AF.Identity).then_inc(sem, 1)
            st_prev = step("a", "a", 1, deps=[stm])(l2ev)
        stLora = st_prev

        def csm(eng, sem):
            for pc in range(SCK):
                for l in range(CPS):
                    sl = slice(pc * SEG + l * L, pc * SEG + (l + 1) * L)
                    eng.tensor_tensor_scan(csb[:, sl], sigw[:, sl], sigw[:, sl], 0.0, ALU.add, ALU.bypass)
            last = None
            for pc in range(SCK):
                last = eng.tensor_scalar(m1t[:, ts(pc, SEG)], agt[:, ts(pc, SEG)],
                                         chccol(1, pc), chccol(7, pc), ALU.mult, ALU.add)
            return last.then_inc(sem, 1)
        stCS = step("v", "v", 1, deps=[stLora])(csm)

        bankpair = [pA, pB]
        st_prevmm = stLm
        st_kev = []
        stProj = None
        last_ev = {0: stMM, 1: stMM}
        for pi, (wS, xm, pname) in enumerate(((wrS, xr_m, "r"), (wkS, xk_m, "k"), (wvS, xv_m, "v"))):
            for oc in range(SCK):
                bi = (pi * SCK + oc) % 2
                bank = bankpair[bi]
                def pmm(eng, sem, wS=wS, xm=xm, oc=oc, bank=bank):
                    last = None
                    for kc in range(NCK):
                        last = eng.matmul(bank[:, 0:SEG], wS[:, kc * CH + oc * P: kc * CH + (oc + 1) * P],
                                          xm[:, ts(kc, SEG)], start=(kc == 0), stop=(kc == NCK - 1))
                    return last.then_inc(sem, 1)
                stm = step("t", "t", 1, deps=[st_prevmm, last_ev[bi]])(pmm)
                if pname == "r":
                    def ev(eng, sem, oc=oc, bank=bank):
                        return eng.tensor_copy(r_s[:, ts(oc, SEG)], bank[:, 0:SEG]).then_inc(sem, 1)
                    sev = step("v", "v", 1, deps=[stm])(ev)
                elif pname == "k":
                    def ev(eng, sem, oc=oc, bank=bank):
                        eng.tensor_scalar_mul(kkn[:, ts(oc, SEG)], bank[:, 0:SEG], chccol(0, oc))
                        return eng.tensor_mul(keff[:, ts(oc, SEG)], bank[:, 0:SEG], m1t[:, ts(oc, SEG)]).then_inc(sem, 1)
                    sev = step("v", "v", 1, deps=[stm, stCS])(ev)
                    st_kev.append(sev)
                else:
                    def ev(eng, sem, oc=oc, bank=bank):
                        return eng.tensor_copy(vfm[:, ts(oc, SEG)], bank[:, 0:SEG]).then_inc(sem, 1)
                    sev = step("v", "v", 1, deps=[stm])(ev)
                st_prevmm = stm
                last_ev[bi] = sev
                stProj = sev

        def kksqf(eng, sem):
            return eng.activation(kksq[:], kkn[:], AF.Square).then_inc(sem, 1)
        stKq = step("a", "a", 1, deps=st_kev)(kksqf)

        st_prevk = stKq
        for pc in range(SCK):
            def knm(eng, sem, pc=pc):
                return eng.matmul(pC[:, 0:SEG], blockUS[:], kksq[:, ts(pc, SEG)], start=True, stop=True).then_inc(sem, 1)
            stm = step("t", "t", 1, deps=[st_prevk])(knm)
            def kne(eng, sem, pc=pc):
                return eng.activation(rsn[:, ts(pc, SEG)], pC[:, 0:SEG], AF.Sqrt).then_inc(sem, 1)
            st_prevk = step("a", "a", 1, deps=[stm])(kne)
        def knfin(eng, sem):
            eng.tensor_scalar_max(rsn[:], rsn[:], 1e-12)
            eng.reciprocal(rsn[:], rsn[:])
            return eng.tensor_mul(kkf[:], kkn[:], rsn[:]).then_inc(sem, 1)
        stKK = step("v", "v", 1, deps=[st_prevk])(knfin)

        for l in range(CPS):
            def exps(eng, sem, l=l):
                E3 = lambda b: b[:].rearrange("p (pc t) -> p pc t", pc=SCK)
                eng.activation(E3(Et), chview(csb, l), AF.Exp, scale=-CDEC)
                eng.activation(E3(Einv), chview(csb, l), AF.Exp, scale=CDEC)
                eng.activation(E3(Eiw), chview(sigw, l), AF.Exp, scale=CDEC)
                return eng.activation(eclT[:], csb[:, (l + 1) * L - 1::SEG], AF.Exp, scale=-CDEC).then_inc(sem, 1)
            stEx = step("a", "a", 1, deps=[stKK, stProj, stCS, st_scan_done])(exps)

            def scaled(eng, sem, l=l):
                AR3 = ARh[:].rearrange("p (pc t) -> p pc t", pc=SCK)
                E3 = lambda b: b[:].rearrange("p (pc t) -> p pc t", pc=SCK)
                eng.scalar_tensor_tensor(tmpE[:], Et[:], -1.0, Eiw[:], ALU.mult, ALU.mult)
                eng.tensor_tensor(AR3[:, :, 0:N], chview(kkf, l), E3(tmpE), ALU.mult)
                eng.tensor_tensor(AR3[:, :, N:2 * N], chview(r_s, l), E3(Et), ALU.mult)
                eng.tensor_tensor(E3(tmpE2), chview(kkf, l), chview(agt, l), ALU.mult)
                eng.tensor_tensor(E3(Bh_t), E3(tmpE2), E3(Einv), ALU.mult)
                eng.tensor_tensor(E3(Kh_t), chview(keff, l), E3(Einv), ALU.mult)
                return eng.tensor_copy(
                    eclB[:].rearrange("p (pc t) -> p pc t", pc=SCK),
                    eclT[:].unsqueeze(2).broadcast_to([P, SCK, N])).then_inc(sem, 1)
            stSc = step("v", "v", 1, deps=[stEx])(scaled)

            def trVBK(eng, sem, l=l):
                last = None
                for src, off in ((None, 0), (Bh_t, 256), (Kh_t, 512)):
                    for h in range(NH):
                        pr = (h % 2) * N
                        if src is None:
                            tile = vfm[pr:pr + N, (h // 2) * SEG + l * L:(h // 2) * SEG + (l + 1) * L]
                        else:
                            tile = hsl(src, h, N)
                        last = eng.transpose(pT[pr:pr + N, off + (h // 2) * N: off + (h // 2 + 1) * N],
                                             tile, i64S[pr:pr + N, :])
                return last.then_inc(sem, 1)
            stTr = step("t", "t", 1, deps=[stSc, prev_tr_t])(trVBK)
            prev_tr_t = stTr

            def evTr(eng, sem):
                eng.tensor_copy(Vtok[:], pT[:, 0:256])
                eng.tensor_copy(Btok[:], pT[:, 256:512])
                return eng.tensor_copy(Ktok[:], pT[:, 512:768]).then_inc(sem, 1)
            stTe = step("v", "v", 1, deps=[stTr])(evTr)

            def ip1(eng, sem):
                last = None
                for h in range(NH):
                    bank = pC if h < 4 else pD
                    col = ((h // 2) % 2) * 128
                    last = eng.matmul(bank[(h % 2) * N:(h % 2) * N + N, col:col + 128],
                                      hsl(Bh_t, h, N), hsl(ARh, h, 2 * N), start=True, stop=True)
                return last.then_inc(sem, 1)
            stI1 = step("t", "t", 1, deps=[stSc, stTe])(ip1)

            def ev1(eng, sem):
                m3 = mcombS[:].unsqueeze(1).broadcast_to([P, 2, P])
                eng.tensor_tensor(ipNR[:, 0:256].rearrange("p (a t) -> p a t", a=2),
                                  pC[:, 0:256].rearrange("p (a t) -> p a t", a=2), m3, ALU.mult)
                return eng.tensor_tensor(ipNR[:, 256:512].rearrange("p (a t) -> p a t", a=2),
                                         pD[:, 0:256].rearrange("p (a t) -> p a t", a=2), m3, ALU.mult).then_inc(sem, 1)
            stV1 = step("v", "v", 1, deps=[stI1])(ev1)

            def ip2(eng, sem):
                last = None
                for h in range(NH):
                    bank = pC if h < 4 else pD
                    col = ((h // 2) % 2) * 128
                    last = eng.matmul(bank[(h % 2) * N:(h % 2) * N + N, col:col + 128],
                                      hsl(Kh_t, h, N), hsl(ARh, h, 2 * N), start=True, stop=True)
                return last.then_inc(sem, 1)
            stI2 = step("t", "t", 1, deps=[stV1])(ip2)

            def ev2(eng, sem):
                m3 = mcombS[:].unsqueeze(1).broadcast_to([P, 2, P])
                eng.tensor_tensor(ipMR[:, 0:256].rearrange("p (a t) -> p a t", a=2),
                                  pC[:, 0:256].rearrange("p (a t) -> p a t", a=2), m3, ALU.mult)
                return eng.tensor_tensor(ipMR[:, 256:512].rearrange("p (a t) -> p a t", a=2),
                                         pD[:, 0:256].rearrange("p (a t) -> p a t", a=2), m3, ALU.mult).then_inc(sem, 1)
            stV2 = step("v", "v", 1, deps=[stI2])(ev2)

            def ip3(eng, sem):
                last = None
                for h in range(NH):
                    last = eng.matmul(pE[(h % 2) * N:(h % 2) * N + N, (h // 2) * N:(h // 2 + 1) * N],
                                      hsl(ARh, h, 2 * N)[:, 0:N], hsl(Bh_t, h, N), start=True, stop=True)
                return last.then_inc(sem, 1)
            stI3 = step("t", "t", 1, deps=[stV2])(ip3)

            def ev3(eng, sem):
                m3 = mlsS[:].unsqueeze(1).broadcast_to([P, SCK, N])
                eng.tensor_tensor(ipNt[:].rearrange("p (a t) -> p a t", a=SCK),
                                  pE[:, 0:256].rearrange("p (a t) -> p a t", a=SCK), m3, ALU.mult)
                i3 = i64S[:].unsqueeze(1).broadcast_to([P, SCK, N])
                ns3 = ipNR[:].rearrange("p (sl t) -> p sl t", sl=SCK)[:, :, 0:N]
                q3 = qT[:].rearrange("p (a t) -> p a t", a=SCK)
                return eng.tensor_tensor(q3, ns3, i3, ALU.add).then_inc(sem, 1)
            stV3 = step("v", "v", 1, deps=[stI3])(ev3)

            curA, curB = None, None
            st_prevN = stV3
            for lvl in range(1, NLVL + 1):
                def powmm(eng, sem, lvl=lvl, curA=curA, curB=curB):
                    last = None
                    for h in range(NH):
                        if lvl == 1:
                            Atile = ipNR[(h % 2) * N:(h % 2) * N + N, (h // 2) * 128:(h // 2) * 128 + N]
                            Btile = hsl(ipNt, h, N)
                        else:
                            Atile = hsl(curA, h, N)
                            Btile = hsl(curB, h, N)
                        col = (h // 2) * N
                        last = eng.matmul(pC[(h % 2) * N:(h % 2) * N + N, col:col + N],
                                          Btile, Atile, start=True, stop=True)
                        last = eng.matmul(pC[(h % 2) * N:(h % 2) * N + N, 256 + col:256 + col + N],
                                          Atile, Btile, start=True, stop=True)
                    return last.then_inc(sem, 1)
                stP = step("t", "t", 1, deps=[st_prevN])(powmm)

                nxtA = neuA if lvl % 2 == 1 else neuA2
                nxtB = neuB if lvl % 2 == 1 else neuB2
                def powev(eng, sem, nxtA=nxtA, nxtB=nxtB):
                    eng.tensor_copy(nxtA[:], pC[:, 0:256])
                    return eng.tensor_copy(nxtB[:], pC[:, 256:512]).then_inc(sem, 1)
                stPe = step("v", "v", 1, deps=[stP])(powev)

                def qmm(eng, sem, nxtB=nxtB):
                    last = None
                    for h in range(NH):
                        last = eng.matmul(pE[(h % 2) * N:(h % 2) * N + N, (h // 2) * N:(h // 2 + 1) * N],
                                          hsl(nxtB, h, N), hsl(qT, h, N), start=True, stop=True)
                    return last.then_inc(sem, 1)
                stQ = step("t", "t", 1, deps=[stPe])(qmm)

                def qev(eng, sem):
                    return eng.tensor_add(qT[:], qT[:], pE[:, 0:256]).then_inc(sem, 1)
                st_prevN = step("v", "v", 1, deps=[stQ])(qev)
                curA, curB = nxtA, nxtB
            stNeu = st_prevN

            def rhsmm(eng, sem):
                last = None
                for h in range(NH):
                    col = (h // 2) * N
                    mka = ipMR[(h % 2) * N:(h % 2) * N + N, (h // 2) * 128:(h // 2) * 128 + N]
                    last = eng.matmul(pF[(h % 2) * N:(h % 2) * N + N, col:col + N],
                                      mka, hsl(Vtok, h, N), start=True, stop=False)
                    last = eng.matmul(pF[(h % 2) * N:(h % 2) * N + N, col:col + N],
                                      hsl(ARh, h, 2 * N)[:, 0:N], hsl(stB, h, N), start=False, stop=True)
                return last.then_inc(sem, 1)
            stRm = step("t", "t", 1, deps=[stV2, stTe, st_scan_done])(rhsmm)

            def rhse(eng, sem):
                return eng.tensor_copy(rhsS[:], pF[:, 0:256]).then_inc(sem, 1)
            stRe = step("v", "v", 1, deps=[stRm])(rhse)

            def umm(eng, sem):
                last = None
                for h in range(NH):
                    last = eng.matmul(pE[(h % 2) * N:(h % 2) * N + N, (h // 2) * N:(h // 2 + 1) * N],
                                      hsl(qT, h, N), hsl(rhsS, h, N), start=True, stop=True)
                return last.then_inc(sem, 1)
            stUm = step("t", "t", 1, deps=[stNeu, stRe])(umm)

            def uev(eng, sem):
                return eng.tensor_copy(uS[:], pE[:, 0:256]).then_inc(sem, 1)
            stUe = step("v", "v", 1, deps=[stUm])(uev)

            def omm(eng, sem):
                last = None
                for h in range(NH):
                    col = 256 + (h // 2) * N
                    rbt = ipNR[(h % 2) * N:(h % 2) * N + N, (h // 2) * 128 + N:(h // 2) * 128 + 2 * N]
                    rkt = ipMR[(h % 2) * N:(h % 2) * N + N, (h // 2) * 128 + N:(h // 2) * 128 + 2 * N]
                    last = eng.matmul(pF[(h % 2) * N:(h % 2) * N + N, col:col + N],
                                      hsl(stB, h, N), hsl(ARh, h, 2 * N)[:, N:2 * N], start=True, stop=False)
                    last = eng.matmul(pF[(h % 2) * N:(h % 2) * N + N, col:col + N],
                                      hsl(uS, h, N), rbt, start=False, stop=False)
                    last = eng.matmul(pF[(h % 2) * N:(h % 2) * N + N, col:col + N],
                                      hsl(Vtok, h, N), rkt, start=False, stop=True)
                return last.then_inc(sem, 1)
            stOm = step("t", "t", 1, deps=[stUe])(omm)

            def oev(eng, sem, l=l):
                return eng.tensor_copy(chview(oseg, l),
                                       pF[:, 256:512].rearrange("p (a t) -> p a t", a=SCK)).then_inc(sem, 1)
            stOe = step("v", "v", 1, deps=[stOm])(oev)

            def smm(eng, sem):
                last = None
                for h in range(NH):
                    col = (h // 2) * N
                    last = eng.matmul(pG[(h % 2) * N:(h % 2) * N + N, col:col + N],
                                      hsl(Ktok, h, N), hsl(Vtok, h, N), start=True, stop=False)
                    last = eng.matmul(pG[(h % 2) * N:(h % 2) * N + N, col:col + N],
                                      hsl(Btok, h, N), hsl(uS, h, N), start=False, stop=True)
                return last.then_inc(sem, 1)
            stSm = step("t", "t", 1, deps=[stUe, stOm])(smm)

            def supd(eng, sem):
                eng.tensor_add(stF[:], stF[:], pG[:])
                eng.tensor_mul(stF[:], stF[:], eclB[:])
                return eng.tensor_copy(stB[:], stF[:]).then_inc(sem, 1)
            st_scan_done = step("v", "v", 1, deps=[stSm, stOe, stSc])(supd)

        def gnsq(eng, sem):
            return eng.activation(o2sq[:], oseg[:], AF.Square).then_inc(sem, 1)
        stGq = step("a", "a", 1, deps=[st_scan_done])(gnsq)

        st_prevg = stGq
        for pc in range(SCK):
            def gmm(eng, sem, pc=pc):
                eng.matmul(pA[:, 0:SEG], blockSS[:], oseg[:, ts(pc, SEG)], start=True, stop=True)
                return eng.matmul(pB[:, 0:SEG], blockSS[:], o2sq[:, ts(pc, SEG)], start=True, stop=True).then_inc(sem, 1)
            stm = step("t", "t", 1, deps=[st_prevg])(gmm)
            def gch(eng, sem, pc=pc):
                eng.tensor_copy(stat1[:], pA[:, 0:SEG])
                eng.tensor_mul(stat2[:], stat1[:], stat1[:])
                eng.tensor_sub(stat2[:], pB[:, 0:SEG], stat2[:])
                return eng.tensor_copy(gnm[:, ts(pc, SEG)], stat1[:]).then_inc(sem, 1)
            stc = step("v", "v", 1, deps=[stm])(gch)
            def gsq2(eng, sem, pc=pc):
                return eng.activation(stat2[:], stat2[:], AF.Sqrt, bias=cEpsGN[:]).then_inc(sem, 1)
            sts2 = step("a", "a", 1, deps=[stc])(gsq2)
            def grc(eng, sem, pc=pc):
                eng.reciprocal(stat2[:], stat2[:])
                return eng.tensor_copy(gnr[:, ts(pc, SEG)], stat2[:]).then_inc(sem, 1)
            st_prevg = step("v", "v", 1, deps=[sts2])(grc)

        def ymk(eng, sem):
            eng.tensor_sub(yseg[:], oseg[:], gnm[:])
            eng.tensor_mul(yseg[:], yseg[:], gnr[:])
            last = None
            for pc in range(SCK):
                last = eng.tensor_scalar(yseg[:, ts(pc, SEG)], yseg[:, ts(pc, SEG)],
                                         chccol(4, pc), chccol(5, pc), ALU.mult, ALU.add)
            eng.tensor_mul(o2sq[:], r_s[:], keff[:])
            for pc in range(SCK):
                last = eng.tensor_scalar_mul(o2sq[:, ts(pc, SEG)], o2sq[:, ts(pc, SEG)], chccol(6, pc))
            return last.then_inc(sem, 1)
        stY1 = step("v", "v", 1, deps=[st_prevg])(ymk)

        st_prevb = stY1
        for pc in range(SCK):
            def bmm(eng, sem, pc=pc):
                return eng.matmul(pA[:, 0:SEG], blockUS[:], o2sq[:, ts(pc, SEG)], start=True, stop=True).then_inc(sem, 1)
            stm = step("t", "t", 1, deps=[st_prevb])(bmm)
            def bev(eng, sem, pc=pc):
                return eng.tensor_copy(rowb[:, ts(pc, SEG)], pA[:, 0:SEG]).then_inc(sem, 1)
            st_prevb = step("v", "v", 1, deps=[stm])(bev)

        def yfin(eng, sem):
            eng.tensor_mul(rowb[:], rowb[:], vfm[:])
            eng.tensor_add(yseg[:], yseg[:], rowb[:])
            return eng.tensor_mul(ygseg[:], yseg[:], gfm[:]).then_inc(sem, 1)
        stYf = step("v", "v", 1, deps=[st_prevb, st_prev_ccst])(yfin)
        st_seg_done = stYf

        if DBG_ON and s == 0:
            def dbg1(eng, sem):
                off = [0]
                last = None
                def dump(buf, n):
                    o = off[0]; off[0] += n
                    return eng.dma_start(out=d_dbg[:, o:o + n], in_=buf)
                last = dump(xn[:], 2048); last.then_inc(sem, 16)
                last = dump(xxb[:], 2048); last.then_inc(sem, 16)
                last = dump(r_s[:], 1024); last.then_inc(sem, 16)
                last = dump(keff[:], 1024); last.then_inc(sem, 16)
                last = dump(kkf[:], 1024); last.then_inc(sem, 16)
                last = dump(agt[:], 1024); last.then_inc(sem, 16)
                last = dump(vfm[:], 1024); last.then_inc(sem, 16)
                last = dump(gfm[:], 1024); last.then_inc(sem, 16)
                last = dump(sigw[:], 1024); last.then_inc(sem, 16)
                last = dump(csb[:], 1024); last.then_inc(sem, 16)
                last = dump(m1t[:], 1024); last.then_inc(sem, 16)
                last = dump(oseg[:], 1024); last.then_inc(sem, 16)
                last = dump(yseg[:], 1024); last.then_inc(sem, 16)
                last = dump(ygseg[:], 1024); last.then_inc(sem, 16)
                last = dump(gnm[:], 1024); last.then_inc(sem, 16)
                last = dump(gnr[:], 1024); last.then_inc(sem, 16)
                last = dump(stB[:], 256); last.then_inc(sem, 16)
                return last
            st_dbg = step("g", "g", 16 * 17, deps=[stYf, st_scan_done, stKK, stCS, stProj, stLora])(dbg1)
            st_x_used = st_dbg
            st_prev_ccst = st_dbg

        def ccst(eng, sem, s=s):
            dst = d_ccin[s // 2].rearrange("p (pc two t) -> p pc two t", two=2, t=SEG)[:, :, s % 2, :]
            return eng.dma_start(out=dst, in_=ygseg[:].rearrange("p (pc t) -> p pc t", pc=SCK)).then_inc(sem, 16)
        stCC = step("q", "q", 16, deps=[stYf])(ccst)
        st_prev_ccst = stCC
        st_x_used = stProj

        if s % 2 == 1:
            def ccgo(eng, sem, s=s):
                return eng.collective_compute(
                    "AllGather", ALU.bypass,
                    ins=[d_ccin[s // 2]], outs=[d_ccout[s // 2]],
                    replica_groups=groups).then_inc(sem, 1)
            st_cc.append(step("g", "g", 1, deps=[stCC])(ccgo))

    st_mixer_end = [st_seg_done, st_scan_done, prev_tr_t, st_cc[-1]]
    AB.reset(_m0B); ARF.reset(_m0F); AI.reset(_m0I)

    sb3 = lambda name, shape, dt=BF16: atake(shape, dt, name)
    woS = sb3("woS", [P, NCK * C])
    ybufA = sb3("ybufA", [P, 2 * SCK * SEG])
    ybufB = sb3("ybufB", [P, 2 * SCK * SEG])
    xrstage = sb3("xrstage", [P, 2 * C])
    xresf = sb3("xresf", [P, NCK * SEG])
    attnf = sb3("attnf", [P, NCK * SEG])
    attsq = sb3("attsq", [P, NCK * SEG])
    w_st1 = sb3("w_st1", [P, SEG], F32)
    w_st2 = sb3("w_st2", [P, SEG], F32)
    w_st2b = sb3("w_st2b", [P, SEG])
    w_st3b = sb3("w_st3b", [P, SEG])
    x1t = sb3("x1t", [P, NCK * SEG])

    @step("g", "w", 16, deps=st_mixer_end)
    def st_wo_load(eng, sem):
        return eng.dma_start(out=woS[:].rearrange("p (kc o) -> p kc o", kc=NCK), in_=d_wo.rearrange("(kc p) o -> p kc o", p=P)).then_inc(sem, 16)

    @step("q", "q", 16, deps=[st_cc[-1]])
    def st_ccmv(eng, sem):
        base = nc.values_load(segbS[0:1, 0:1], engines=[mybir.EngineType.SP], min_val=0, max_val=1, skip_runtime_bounds_check=True)
        src_f = d_ccout.rearrange("a r p f -> a (r p f)")
        dst_f = d_ccmine.rearrange("a r p f -> a (r p f)")
        return eng.dma_start(out=dst_f, in_=src_f[bass.ts(base, NEX // 2), :]).then_inc(sem, 16)

    st_prev_wo = st_cc[-1]
    prev_tr2 = prev_tr_t
    st_ybuf_used = [st_cc[-1], st_cc[-1]]
    stX1 = None
    for i in range(OSEG):
        ybuf = ybufA if i % 2 == 0 else ybufB

        def ldy(eng, sem, i=i, ybuf=ybuf):
            for r in range(2):
                src2 = d_ccmine[i // 2, r].rearrange(
                    "p (pc two t) -> p pc two t", two=2, t=SEG)[:, :, i % 2, :]
                eng.dma_start(out=ybuf[:, r * SCK * SEG:(r + 1) * SCK * SEG],
                              in_=src2).then_inc(sem, 16)
            return None
        stLy = step("q", "q", 32, deps=[st_ccmv, st_ybuf_used[i % 2]])(ldy)

        def ldxr(eng, sem, i=i):
            return eng.dma_start(out=xrstage[:].rearrange("p (tt c) -> p tt c", tt=2),
                                 in_=d_xres[ts(i, SEG), :].rearrange("(tt p) c -> p tt c", p=P)).then_inc(sem, 16)
        stXr = step("g", "w", 16, deps=[st_prev_wo])(ldxr)

        def trxr1(eng, sem):
            last = None
            for kc in range(NCK):
                last = eng.transpose(pT[:, ts(kc, P)], xrstage[:, kc * P:(kc + 1) * P], identS[:])
            return last.then_inc(sem, 1)
        def evxr1(eng, sem):
            return eng.tensor_copy(
                xresf[:].rearrange("p (kc t) -> p kc t", kc=NCK)[:, :, 0:P],
                pT[:].rearrange("p (kc t) -> p kc t", kc=NCK)).then_inc(sem, 1)
        def trxr2(eng, sem):
            last = None
            for kc in range(NCK):
                last = eng.transpose(pT[:, ts(kc, P)], xrstage[:, C + kc * P: C + (kc + 1) * P], identS[:])
            return last.then_inc(sem, 1)
        def evxr2(eng, sem):
            return eng.tensor_copy(
                xresf[:].rearrange("p (kc t) -> p kc t", kc=NCK)[:, :, P:SEG],
                pT[:].rearrange("p (kc t) -> p kc t", kc=NCK)).then_inc(sem, 1)
        stT1 = step("t", "t", 1, deps=[stXr, prev_tr2])(trxr1)
        stEv1 = step("v", "v", 1, deps=[stT1])(evxr1)
        stT2 = step("t", "t", 1, deps=[stEv1])(trxr2)
        stEv2 = step("v", "v", 1, deps=[stT2])(evxr2)
        prev_tr2 = stT2

        st_aev = []
        if i == 0:
            _wo_lastev = {0: st_cc[-1], 1: st_cc[-1]}
        for oc in range(NCK):
            bank = pA if oc % 2 == 0 else pB
            def womm(eng, sem, oc=oc, bank=bank, ybuf=ybuf):
                last = None
                for kc in range(NCK):
                    last = eng.matmul(bank[:, 0:SEG], woS[:, kc * C + oc * P: kc * C + (oc + 1) * P],
                                      ybuf[:, ts(kc, SEG)], start=(kc == 0), stop=(kc == NCK - 1))
                return last.then_inc(sem, 1)
            stm = step("t", "t", 1, deps=[stLy, st_wo_load, stEv2, _wo_lastev[oc % 2]])(womm)
            def aev(eng, sem, oc=oc, bank=bank):
                return eng.tensor_copy(attnf[:, ts(oc, SEG)], bank[:, 0:SEG]).then_inc(sem, 1)
            st_aev.append(step("v", "v", 1, deps=[stm])(aev))
            _wo_lastev[oc % 2] = st_aev[-1]
        st_ybuf_used[i % 2] = st_aev[-1]

        def asq(eng, sem):
            return eng.activation(attsq[:], attnf[:], AF.Square).then_inc(sem, 1)
        stAq = step("a", "a", 1, deps=[st_aev[-1]])(asq)

        def lnm(eng, sem):
            for kc in range(NCK):
                eng.matmul(pC[:, 0:SEG], onesSS[:], attnf[:, ts(kc, SEG)], start=(kc == 0), stop=(kc == NCK - 1))
            last = None
            for kc in range(NCK):
                last = eng.matmul(pD[:, 0:SEG], onesSS[:], attsq[:, ts(kc, SEG)], start=(kc == 0), stop=(kc == NCK - 1))
            return last.then_inc(sem, 1)
        stLm2 = step("t", "t", 1, deps=[stAq])(lnm)

        def lnc1(eng, sem):
            eng.tensor_copy(w_st1[:], pC[:, 0:SEG])
            eng.tensor_mul(w_st2[:], w_st1[:], w_st1[:])
            return eng.tensor_sub(w_st2[:], pD[:, 0:SEG], w_st2[:]).then_inc(sem, 1)
        stLc1 = step("v", "v", 1, deps=[stLm2])(lnc1)

        def lns(eng, sem):
            return eng.activation(w_st2[:], w_st2[:], AF.Sqrt, bias=cEpsLN[:]).then_inc(sem, 1)
        stLs = step("a", "a", 1, deps=[stLc1])(lns)

        def lnc2(eng, sem):
            eng.reciprocal(w_st2[:], w_st2[:])
            eng.tensor_copy(w_st2b[:], w_st2[:])
            eng.tensor_mul(w_st2[:], w_st1[:], w_st2[:])
            eng.tensor_copy(w_st3b[:], w_st2[:])
            a3 = attnf[:].rearrange("p (kc t) -> p kc t", kc=NCK)
            x3 = x1t[:].rearrange("p (kc t) -> p kc t", kc=NCK)
            rb = w_st2b[:].unsqueeze(1).broadcast_to([P, NCK, SEG])
            mb = w_st3b[:].unsqueeze(1).broadcast_to([P, NCK, SEG])
            eng.tensor_tensor(x3, a3, rb, ALU.mult)
            eng.tensor_tensor(x3, x3, mb, ALU.subtract)
            return eng.tensor_add(x1t[:], x1t[:], xresf[:]).then_inc(sem, 1)
        stX1 = step("v", "v", 1, deps=[stLs, stEv2])(lnc2)

        if DBG_ON and i == 0:
            def dbg2(eng, sem, ybuf=ybuf):
                eng.dma_start(out=d_dbg[:, 16640:18688], in_=ybuf[:]).then_inc(sem, 16)
                eng.dma_start(out=d_dbg[:, 18688:20736], in_=attnf[:]).then_inc(sem, 16)
                eng.dma_start(out=d_dbg[:, 20736:22784], in_=x1t[:]).then_inc(sem, 16)
                return eng.dma_start(out=d_dbg[:, 22784:24832], in_=xresf[:]).then_inc(sem, 16)
            st_dbg2 = step("g", "g", 16 * 4, deps=[stX1, st_aev[-1], stEv2])(dbg2)
            st_prev_wo = st_dbg2

        def stx1(eng, sem, i=i):
            return eng.dma_start(out=d_x1.rearrange("(kc p) t -> p kc t", p=P)[:, :, ts(i, SEG)],
                                 in_=x1t[:].rearrange("p (kc t) -> p kc t", kc=NCK)).then_inc(sem, 16)
        st_prev_wo = step("q", "q", 16, deps=[stX1])(stx1)

    st_wo_end = [st_prev_wo, stX1, prev_tr2]
    AB.reset(_m0B); ARF.reset(_m0F); AI.reset(_m0I)

    sb4 = lambda name, shape, dt=BF16: atake(shape, dt, name)
    x1s = sb4("x1s", [P, NCK * 512])
    x1sq = sb4("x1sq", [P, NCK * 512])
    f_st1 = sb4("f_st1", [P, 512], F32)
    f_st2 = sb4("f_st2", [P, 512], F32)
    f_rho = sb4("f_rho", [P, 512])
    f_mrho = sb4("f_mrho", [P, 512])
    tqf = sb4("tqf", [P, NCK * 512], F32)
    ic32 = sb4("ic32", [P, 2048], I32)
    f_c1 = sb4("f_c1", [P, 2048])
    f_fp = sb4("f_fp", [P, 2048])
    qsB = sb4("qsB", [P, OT])
    dscB = sb4("dscB", [P, OT])
    sxB = sb4("sxB", [P, OT])
    rowmx = sb4("rowmx", [P, OT])

    selmS = sb4("selmS", [P, 512])
    mxpad = sb4("mxpad", [P, 128])
    sxt_sb = sb4("sxt_sb", [P, 128])
    mxq = sb4("mxq", [P, 4], F32)
    f2b = x1sq[:, 2560:3072]
    tposeb = x1sq[:, 2048:2560]

    @step("q", "q", 16, deps=st_wo_end)
    def st_selm(eng, sem):
        return eng.dma_start(out=selmS[:], in_=d_selm).then_inc(sem, 16)

    @step("v", "v", 1, deps=st_wo_end)
    def st_mxz(eng, sem):
        return eng.memset(mxpad[:], 0.0).then_inc(sem, 1)

    def emit_pmax_bcast(dep_start, partial_ap, out_fin_fn, extra_t_deps=()):
        def qtr(eng, sem):
            last = None
            for c in range(4):
                last = eng.transpose(pT[:, ts(c, P)], partial_ap[:, ts(c, P)], identS[:])
            return last.then_inc(sem, 1)
        stqt = step("t", "t", 1, deps=[dep_start, *extra_t_deps])(qtr)
        def qred(eng, sem):
            eng.tensor_copy(tposeb, pT[:, 0:512])
            eng.tensor_reduce(mxq[:], tposeb.rearrange("p (a f) -> p a f", a=4),
                              AX.X, ALU.max, apply_absolute_value=True)
            return eng.tensor_reduce(mxpad[:, 0:4], tposeb.rearrange("p (a f) -> p a f", a=4),
                              AX.X, ALU.max, apply_absolute_value=True).then_inc(sem, 1)
        stqr = step("v", "v", 1, deps=[stqt])(qred)
        def qtr2(eng, sem):
            return eng.transpose(pT[:, 512:640], mxpad[:], identS[:]).then_inc(sem, 1)
        stqt2 = step("t", "t", 1, deps=[stqr])(qtr2)
        def qev(eng, sem):
            return eng.tensor_copy(sxt_sb[:], pT[:, 512:640]).then_inc(sem, 1)
        stqe = step("v", "v", 1, deps=[stqt2])(qev)
        def qbc(eng, sem):
            last = None
            for c in range(4):
                last = eng.matmul(pE[:, ts(c, P)], selmS[:, ts(c, P)],
                                  sxt_sb[:, :], start=True, stop=True)
            return last.then_inc(sem, 1)
        stqb = step("t", "t", 1, deps=[stqe, st_selm])(qbc)
        def qfin(eng, sem):
            return out_fin_fn(eng).then_inc(sem, 1)
        return step("v", "v", 1, deps=[stqb])(qfin)

    _m1B, _m1F, _m1I = AB.mark(), ARF.mark(), AI.mark()
    sb4b = lambda name, shape, dt=BF16: atake(shape, dt, name)
    x2q = sb4b("x2q", [P, NCK * OT])
    x2t = sb4b("x2t", [P, NCK * 512])
    invgB = sb4b("invgB", [P, OT])
    wgt = sb4b("wgt", [P, NCK * P])
    wgt2 = sb4b("wgt2", [P, NCK * P])
    wut = sb4b("wut", [P, NCK * P])
    wut2 = sb4b("wut2", [P, NCK * P])
    gpre = sb4b("gpre", [P, 512])
    silu_s = sb4b("silu_s", [P, 512])
    up_s = sb4b("up_s", [P, 512])
    h_s = sb4b("h_s", [P, 512])

    prev_f = st_wo_end[0]
    prev_ft = st_wo_end[2]
    st_x2_done = []
    DBG3H = []
    for tsl in range(OT // 512):
        def ldx1(eng, sem, tsl=tsl):
            return eng.dma_start(out=x1s[:].rearrange("p (kc t) -> p kc t", kc=NCK),
                                 in_=d_x1.rearrange("(kc p) t -> p kc t", p=P)[:, :, ts(tsl, 512)]).then_inc(sem, 16)
        stq = step("q", "q", 16, deps=st_wo_end + [prev_f] + DBG3H)(ldx1)

        def x1sqf(eng, sem):
            return eng.activation(x1sq[:], x1s[:], AF.Square).then_inc(sem, 1)
        sta = step("a", "a", 1, deps=[stq])(x1sqf)

        def lnm3(eng, sem):
            for kc in range(NCK):
                eng.matmul(pA[:, 0:512], onesSS[:], x1s[:, ts(kc, 512)], start=(kc == 0), stop=(kc == NCK - 1))
            last = None
            for kc in range(NCK):
                last = eng.matmul(pB[:, 0:512], onesSS[:], x1sq[:, ts(kc, 512)], start=(kc == 0), stop=(kc == NCK - 1))
            return last.then_inc(sem, 1)
        stm = step("t", "t", 1, deps=[sta, prev_ft])(lnm3)
        prev_ft = stm

        def c1f(eng, sem):
            eng.tensor_copy(f_st1[:], pA[:, 0:512])
            eng.tensor_mul(f_st2[:], f_st1[:], f_st1[:])
            return eng.tensor_sub(f_st2[:], pB[:, 0:512], f_st2[:]).then_inc(sem, 1)
        stc1 = step("v", "v", 1, deps=[stm])(c1f)

        def sqf2(eng, sem):
            return eng.activation(f_st2[:], f_st2[:], AF.Sqrt, bias=cEpsLN[:]).then_inc(sem, 1)
        sts = step("a", "a", 1, deps=[stc1])(sqf2)

        def c2f(eng, sem, tsl=tsl):
            eng.reciprocal(f_st2[:], f_st2[:])
            eng.tensor_copy(f_rho[:], f_st2[:])
            eng.tensor_mul(f_st2[:], f_st1[:], f_st2[:])
            eng.tensor_copy(f_mrho[:], f_st2[:])
            x3 = x1s[:].rearrange("p (kc t) -> p kc t", kc=NCK)
            o3 = x2t[:].rearrange("p (kc t) -> p kc t", kc=NCK)
            rb = f_rho[:].unsqueeze(1).broadcast_to([P, NCK, 512])
            mb = f_mrho[:].unsqueeze(1).broadcast_to([P, NCK, 512])
            eng.tensor_tensor(o3, x3, rb, ALU.mult)
            eng.tensor_tensor(o3, o3, mb, ALU.subtract)
            eng.scalar_tensor_tensor(f_st2[:], x2t[:, 0:512], -1.0, x2t[:, 0:512], ALU.mult, ALU.max)
            last = None
            for kc in range(1, NCK):
                eng.scalar_tensor_tensor(x1sq[:, 0:512], x2t[:, ts(kc, 512)], -1.0, x2t[:, ts(kc, 512)], ALU.mult, ALU.max)
                last = eng.tensor_tensor(f_st2[:], f_st2[:], x1sq[:, 0:512], ALU.max)
            return last.then_inc(sem, 1)
        stc2 = step("v", "v", 1, deps=[sts])(c2f)

        def q1cast(eng, sem):
            return eng.tensor_copy(f2b, f_st2[:]).then_inc(sem, 1)
        stqc = step("v", "v", 1, deps=[stc2, st_mxz])(q1cast)
        stpm = emit_pmax_bcast(stqc, f2b,
                               lambda eng: eng.tensor_copy(f_st2[:], pE[:, 0:512]))

        def lad0(eng, sem, tsl=tsl):
            eng.tensor_scalar_max(f_st2[:], f_st2[:], 1e-5)
            eng.reciprocal(f_st2[:], f_st2[:])
            eng.tensor_scalar_mul(f_st2[:], f_st2[:], 127.0)
            eng.tensor_copy(sxB[:, ts(tsl, 512)], f_st2[:])
            x23 = x2t[:].rearrange("p (kc t) -> p kc t", kc=NCK)
            sx3 = f_st2[:].unsqueeze(1).broadcast_to([P, NCK, 512])
            t3 = tqf[:].rearrange("p (kc t) -> p kc t", kc=NCK)
            return eng.tensor_tensor(t3, x23, sx3, ALU.mult).then_inc(sem, 1)
        st_half = step("v", "v", 1, deps=[stpm])(lad0)

        if DBG_ON and tsl == 0:
            def dbg3(eng, sem):
                eng.dma_start(out=d_dbg[:, 24832:28928], in_=x2t[:]).then_inc(sem, 16)
                eng.dma_start(out=d_dbg[:, 28928:29440], in_=f_st2[:]).then_inc(sem, 16)
                eng.dma_start(out=d_dbg[:, 33536:34048], in_=f2b).then_inc(sem, 16)
                eng.dma_start(out=d_dbg[:, 34048:34560], in_=tposeb).then_inc(sem, 16)
                eng.dma_start(out=d_dbg[:, 34560:34564], in_=mxq[:]).then_inc(sem, 16)
                eng.dma_start(out=d_dbg[:, 34564:34692], in_=mxpad[:]).then_inc(sem, 16)
                eng.dma_start(out=d_dbg[:, 34692:34820], in_=sxt_sb[:]).then_inc(sem, 16)
                return eng.dma_start(out=d_dbg[:, 34820:35332], in_=selmS[:]).then_inc(sem, 16)
            st_dbg3 = step("g", "g", 16 * 8, deps=[st_half])(dbg3)
            DBG3H.append(st_dbg3)
        for hf in range(2):
            def lad1(eng, sem, hf=hf):
                return eng.tensor_single_scalar(ic32[:], tqf[:, ts(hf, 2048)], -1.0, ALU.add).then_inc(sem, 1)
            stld = step("v", "v", 1, deps=[st_half])(lad1)
            def fpc(eng, sem):
                return eng.tensor_copy(f_fp[:], ic32[:]).then_inc(sem, 1)
            stfp = step("v", "v", 1, deps=[stld])(fpc)
            def lad2(eng, sem, tsl=tsl, hf=hf):
                eng.scalar_tensor_tensor(f_c1[:], tqf[:, ts(hf, 2048)], 1.5, f_fp[:], ALU.subtract, ALU.is_gt)
                eng.scalar_tensor_tensor(x1sq[:, 0:2048], tqf[:, ts(hf, 2048)], 0.5, f_fp[:], ALU.subtract, ALU.is_ge)
                eng.tensor_add(f_c1[:], f_c1[:], x1sq[:, 0:2048])
                x2q3 = x2q[:].rearrange("p (kc t) -> p kc t", kc=NCK)[:, 4 * hf:4 * hf + 4, ts(tsl, 512)]
                f3 = f_c1[:].rearrange("p (kc t) -> p kc t", kc=4)
                fp3 = f_fp[:].rearrange("p (kc t) -> p kc t", kc=4)
                return eng.tensor_tensor(x2q3, f3, fp3, ALU.add).then_inc(sem, 1)
            st_half = step("v", "v", 1, deps=[stfp])(lad2)
        prev_f = st_half
        st_x2_done.append(prev_f)

    if DBG_ON:
        @step("g", "g", 16, deps=[prev_f])
        def st_dbg4(eng, sem):
            return eng.dma_start(out=d_dbg[:, 29440:33536].rearrange("p (kc t) -> p kc t", kc=NCK), in_=x2q[:].rearrange("p (kc t) -> p kc t", kc=NCK)[:, :, 0:512]).then_inc(sem, 16)

    @step("v", "v", 1, deps=[prev_f, st_neg])
    def st_inv(eng, sem):
        eng.reciprocal(tqf[:], sxB[:])
        return eng.tensor_scalar_mul(invgB[:], tqf[:], winv[:, 0:1]).then_inc(sem, 1)

    prev_hst = st_inv
    prev_wld = [st_inv, st_inv]
    for m in range(FI // P):
        wg_b = wgt if m % 2 == 0 else wgt2
        wu_b = wut if m % 2 == 0 else wut2
        def ldw(eng, sem, m=m, wg_b=wg_b, wu_b=wu_b):
            eng.dma_start(out=wg_b[:].rearrange("p (kc o) -> p kc o", kc=NCK),
                          in_=d_qwg.rearrange("(kc p) o -> p kc o", p=P)[:, :, ts(m, P)]).then_inc(sem, 16)
            return eng.dma_start(out=wu_b[:].rearrange("p (kc o) -> p kc o", kc=NCK),
                                 in_=d_qwu.rearrange("(kc p) o -> p kc o", p=P)[:, :, ts(m, P)]).then_inc(sem, 16)
        stw = step("q", "q", 32, deps=[prev_wld[m % 2], prev_store])(ldw)
        prev_wld[m % 2] = stw

        for tsl2 in range(OT // 512):
            bankg = pA if tsl2 % 2 == 0 else pB
            banku = pC if tsl2 % 2 == 0 else pD
            def gumm(eng, sem, m=m, tsl2=tsl2, wg_b=wg_b, wu_b=wu_b, bankg=bankg, banku=banku):
                for kc in range(NCK):
                    eng.matmul(bankg[:, 0:512], wg_b[:, ts(kc, P)],
                               x2q[:, kc * OT + tsl2 * 512: kc * OT + (tsl2 + 1) * 512],
                               start=(kc == 0), stop=(kc == NCK - 1))
                last = None
                for kc in range(NCK):
                    last = eng.matmul(banku[:, 0:512], wu_b[:, ts(kc, P)],
                                      x2q[:, kc * OT + tsl2 * 512: kc * OT + (tsl2 + 1) * 512],
                                      start=(kc == 0), stop=(kc == NCK - 1))
                return last.then_inc(sem, 1)
            stg = step("t", "t", 1, deps=[stw, st_x2_done[tsl2], prev_hst, prev_ft])(gumm)
            prev_ft = stg

            def gev(eng, sem, tsl2=tsl2, bankg=bankg, m=m):
                if DBG_ON and m == 0 and tsl2 == 0:
                    eng.tensor_copy(x1sq[:, 1024:1536], bankg[:, 0:512])
                    eng.tensor_copy(x1sq[:, 1536:2048], invgB[:, 0:512])
                return eng.tensor_mul(gpre[:], bankg[:, 0:512], invgB[:, ts(tsl2, 512)]).then_inc(sem, 1)
            stge = step("v", "v", 1, deps=[stg])(gev)

            def sil(eng, sem):
                return eng.activation(silu_s[:], gpre[:], AF.Silu).then_inc(sem, 1)
            stsl = step("a", "a", 1, deps=[stge])(sil)

            def hmk(eng, sem, tsl2=tsl2, banku=banku, m=m):
                eng.tensor_copy(up_s[:], banku[:, 0:512])
                eng.tensor_mul(h_s[:], silu_s[:], up_s[:])
                if m == 0:
                    return eng.scalar_tensor_tensor(rowmx[:, ts(tsl2, 512)], h_s[:], -1.0,
                                                    h_s[:], ALU.mult, ALU.max).then_inc(sem, 1)
                eng.scalar_tensor_tensor(x1sq[:, 0:512], h_s[:], -1.0, h_s[:], ALU.mult, ALU.max)
                return eng.tensor_tensor(rowmx[:, ts(tsl2, 512)], rowmx[:, ts(tsl2, 512)],
                                         x1sq[:, 0:512], ALU.max).then_inc(sem, 1)
            sth = step("v", "v", 1, deps=[stsl])(hmk)

            if DBG_ON and m == 0 and tsl2 == 0:
                def dbg6(eng, sem):
                    eng.dma_start(out=d_dbg[:, 46100:46612], in_=gpre[:]).then_inc(sem, 16)
                    eng.dma_start(out=d_dbg[:, 48148:48660], in_=x1sq[:, 1024:1536]).then_inc(sem, 16)
                    eng.dma_start(out=d_dbg[:, 48660:49172], in_=x1sq[:, 1536:2048]).then_inc(sem, 16)
                    eng.dma_start(out=d_dbg[:, 46612:47124], in_=silu_s[:]).then_inc(sem, 16)
                    eng.dma_start(out=d_dbg[:, 47124:47636], in_=up_s[:]).then_inc(sem, 16)
                    return eng.dma_start(out=d_dbg[:, 47636:48148], in_=h_s[:]).then_inc(sem, 16)
                st_dbg6 = step("g", "g", 16 * 6, deps=[sth])(dbg6)
                DBG3H.append(st_dbg6)

            def hst(eng, sem, m=m, tsl2=tsl2):
                return eng.dma_start(out=d_hbuf[ts(m, P), ts(tsl2, 512)], in_=h_s[:]).then_inc(sem, 16)
            prev_hst = step("q", "q", 16, deps=[sth])(hst)

    st_shred = prev_hst
    for _sl in range(OT // 512):
        def shfin(eng, sem=None, _sl=_sl):
            return eng.tensor_copy(rowmx[:, ts(_sl, 512)], pE[:, 0:512])
        st_shred = emit_pmax_bcast(st_shred, rowmx[:, ts(_sl, 512)],
                                   lambda eng, _sl=_sl: eng.tensor_copy(rowmx[:, ts(_sl, 512)], pE[:, 0:512]))

    @step("v", "v", 1, deps=[st_shred])
    def st_sh(eng, sem):
        eng.reciprocal(tqf[:], sxB[:])
        eng.tensor_scalar_mul(rowmx[:], rowmx[:], winv[:, 1:2])
        eng.tensor_mul(rowmx[:], rowmx[:], tqf[:])
        eng.tensor_scalar_max(rowmx[:], rowmx[:], 1e-5)
        eng.reciprocal(dscB[:], rowmx[:])
        eng.tensor_scalar_mul(dscB[:], dscB[:], 127.0)
        eng.tensor_scalar_mul(qsB[:], dscB[:], winv[:, 1:2])
        eng.tensor_mul(qsB[:], qsB[:], tqf[:])
        eng.reciprocal(dscB[:], dscB[:])
        return eng.tensor_scalar_mul(dscB[:], dscB[:], winv[:, 2:3]).then_inc(sem, 1)
    st_rows = st_sh

    AB.reset(_m1B); ARF.reset(_m1F); AI.reset(_m1I)

    sb4c = lambda name, shape, dt=BF16: atake(shape, dt, name)
    wdtS = sb4c("wdtS", [P, 32 * P])
    wdtS2 = sb4c("wdtS2", [P, 32 * P])
    hq_s = sb4c("hq_s", [P, 32 * 512])
    hld = sb4c("hld", [P, 512])
    dwn = sb4c("dwn", [P, NCK * 512])
    dwsq = sb4c("dwsq", [P, NCK * 512])
    outf = sb4c("outf", [P, NCK * 512], F32)

    st_out_dma = []
    prev_dn = st_rows
    for tsl3 in range(OT // 512):
        prev_l = prev_dn
        for ic in range(FI // P):
            def ldh(eng, sem, ic=ic, tsl3=tsl3):
                return eng.dma_start(out=hld[:], in_=d_hbuf[ts(ic, P), ts(tsl3, 512)]).then_inc(sem, 16)
            sth2 = step("q", "q", 16, deps=[prev_l, st_rows])(ldh)
            def lad1b(eng, sem, ic=ic, tsl3=tsl3):
                eng.tensor_mul(tqf[:, 0:512], hld[:], qsB[:, ts(tsl3, 512)])
                return eng.tensor_single_scalar(ic32[:, 0:512], tqf[:, 0:512], -1.0, ALU.add).then_inc(sem, 1)
            stl1 = step("v", "v", 1, deps=[sth2])(lad1b)
            def fpb(eng, sem):
                return eng.tensor_copy(f_fp[:, 0:512], ic32[:, 0:512]).then_inc(sem, 1)
            stf2 = step("v", "v", 1, deps=[stl1])(fpb)
            def lad2b(eng, sem, ic=ic):
                eng.scalar_tensor_tensor(f_c1[:, 0:512], tqf[:, 0:512], 1.5, f_fp[:, 0:512], ALU.subtract, ALU.is_gt)
                eng.scalar_tensor_tensor(x1sq[:, 0:512], tqf[:, 0:512], 0.5, f_fp[:, 0:512], ALU.subtract, ALU.is_ge)
                eng.tensor_add(f_c1[:, 0:512], f_c1[:, 0:512], x1sq[:, 0:512])
                return eng.tensor_add(hq_s[:, ts(ic, 512)], f_c1[:, 0:512], f_fp[:, 0:512]).then_inc(sem, 1)
            prev_l = step("v", "v", 1, deps=[stf2])(lad2b)
        st_hq_done = prev_l

        last_dev = {0: st_rows, 1: st_rows}
        for oc in range(NCK):
            wdb = wdtS if oc % 2 == 0 else wdtS2
            def ldwd(eng, sem, oc=oc, wdb=wdb):
                return eng.dma_start(out=wdb[:].rearrange("p (ic o) -> p ic o", ic=32),
                                     in_=d_qwd.rearrange("(ic p) o -> p ic o", p=P)[:, :, ts(oc, P)]).then_inc(sem, 16)
            stwd = step("q", "q", 16, deps=[st_hq_done, prev_dn])(ldwd)
            bank = pA if oc % 2 == 0 else pB
            def dmm(eng, sem, oc=oc, wdb=wdb, bank=bank):
                last = None
                for ic in range(FI // P):
                    last = eng.matmul(bank[:, 0:512], wdb[:, ts(ic, P)], hq_s[:, ts(ic, 512)],
                                      start=(ic == 0), stop=(ic == FI // P - 1))
                return last.then_inc(sem, 1)
            stdm = step("t", "t", 1, deps=[stwd, st_hq_done, prev_ft, last_dev[oc % 2]])(dmm)
            prev_ft = stdm
            def dev(eng, sem, oc=oc, bank=bank, tsl3=tsl3):
                return eng.tensor_mul(dwn[:, ts(oc, 512)], bank[:, 0:512], dscB[:, ts(tsl3, 512)]).then_inc(sem, 1)
            prev_dn = step("v", "v", 1, deps=[stdm])(dev)
            last_dev[oc % 2] = prev_dn
        st_down = prev_dn

        def dsq(eng, sem):
            return eng.activation(dwsq[:], dwn[:], AF.Square).then_inc(sem, 1)
        stDq = step("a", "a", 1, deps=[st_down])(dsq)

        def lnm4(eng, sem):
            for kc in range(NCK):
                eng.matmul(pC[:, 0:512], onesSS[:], dwn[:, ts(kc, 512)], start=(kc == 0), stop=(kc == NCK - 1))
            last = None
            for kc in range(NCK):
                last = eng.matmul(pD[:, 0:512], onesSS[:], dwsq[:, ts(kc, 512)], start=(kc == 0), stop=(kc == NCK - 1))
            return last.then_inc(sem, 1)
        stm4 = step("t", "t", 1, deps=[stDq, prev_ft])(lnm4)
        prev_ft = stm4

        def c1f4(eng, sem):
            eng.tensor_copy(f_st1[:], pC[:, 0:512])
            eng.tensor_mul(f_st2[:], f_st1[:], f_st1[:])
            return eng.tensor_sub(f_st2[:], pD[:, 0:512], f_st2[:]).then_inc(sem, 1)
        stc14 = step("v", "v", 1, deps=[stm4])(c1f4)

        def sqf4(eng, sem):
            return eng.activation(f_st2[:], f_st2[:], AF.Sqrt, bias=cEpsLN[:]).then_inc(sem, 1)
        sts4 = step("a", "a", 1, deps=[stc14])(sqf4)

        def ldx1b(eng, sem, tsl3=tsl3):
            return eng.dma_start(out=x1s[:].rearrange("p (kc t) -> p kc t", kc=NCK),
                                 in_=d_x1.rearrange("(kc p) t -> p kc t", p=P)[:, :, ts(tsl3, 512)]).then_inc(sem, 16)
        stx1b = step("q", "q", 16, deps=[st_down])(ldx1b)

        def fin(eng, sem, tsl3=tsl3):
            eng.reciprocal(f_st2[:], f_st2[:])
            eng.tensor_copy(f_rho[:], f_st2[:])
            eng.tensor_mul(f_st2[:], f_st1[:], f_st2[:])
            eng.tensor_copy(f_mrho[:], f_st2[:])
            d3 = dwn[:].rearrange("p (kc t) -> p kc t", kc=NCK)
            o3 = outf[:].rearrange("p (kc t) -> p kc t", kc=NCK)
            rb = f_rho[:].unsqueeze(1).broadcast_to([P, NCK, 512])
            mb = f_mrho[:].unsqueeze(1).broadcast_to([P, NCK, 512])
            eng.tensor_tensor(o3, d3, rb, ALU.mult)
            eng.tensor_tensor(o3, o3, mb, ALU.subtract)
            return eng.tensor_add(outf[:], outf[:], x1s[:]).then_inc(sem, 1)
        stfin = step("v", "v", 1, deps=[sts4, stx1b])(fin)

        if DBG_ON and tsl3 == 0:
            def dbg5(eng, sem):
                eng.dma_start(out=d_dbg[:, 35332:39428], in_=dwn[:]).then_inc(sem, 16)
                eng.dma_start(out=d_dbg[:, 39428:43524], in_=outf[:]).then_inc(sem, 16)
                eng.dma_start(out=d_dbg[:, 43524:44036], in_=qsB[:, 0:512]).then_inc(sem, 16)
                eng.dma_start(out=d_dbg[:, 44036:44548], in_=dscB[:, 0:512]).then_inc(sem, 16)
                eng.dma_start(out=d_dbg[:, 44548:45572], in_=hq_s[:, 0:1024]).then_inc(sem, 16)
                eng.dma_start(out=d_dbg[:, 45572:46084], in_=rowmx[:, 0:512]).then_inc(sem, 16)
                eng.dma_start(out=d_dbg[:, 46084:46092], in_=wacc[:]).then_inc(sem, 16)
                return eng.dma_start(out=d_dbg[:, 46092:46100], in_=fsc[:]).then_inc(sem, 16)
            st_dbg5 = step("g", "g", 16 * 8, deps=[stfin])(dbg5)
            prev_dn = st_dbg5

        def outdma(eng, sem, tsl3=tsl3):
            return eng.dma_start(out=d_out.rearrange("(kc p) t -> p kc t", p=P)[:, :, ts(tsl3, 512)],
                                 in_=outf[:].rearrange("p (kc t) -> p kc t", kc=NCK)).then_inc(sem, 16)
        st_out_dma.append(step("q", "q", 16, deps=[stfin])(outdma))
        prev_dn = stfin


    @step("q", "q", 0, deps=[st_out_dma[-1]])
    def st_fin(eng, sem):
        return eng.wait_ge(sems["q"], CNT["q"])

    with nc.allow_low_precision(reason="bf16 activations with fp32 stats by design"), nc.Block() as blk:
        def run_queue(eng, qname):
            for d, fn, semname, incs in PLAN[qname]:
                for de, dc in d.items():
                    eng.wait_ge(sems[de], dc)
                fn(eng, sems[semname])
        blk.sync(lambda eng: run_queue(eng, "q"))
        blk.tensor(lambda eng: run_queue(eng, "t"))
        blk.vector(lambda eng: run_queue(eng, "v"))
        blk.scalar(lambda eng: run_queue(eng, "a"))
        blk.gpsimd(lambda eng: run_queue(eng, "g"))

    es.close()
    return nc


_CACHE = {}


def kernel(**inputs):
    if "nc" not in _CACHE:
        _CACHE["nc"] = build_nc()
    nc = _CACHE["nc"]
    import os as _os
    if _os.environ.get("STEP_LIMIT"):
        print("STEP_LIMIT ACTIVE:", _os.environ["STEP_LIMIT"])

    f32 = np.float32
    import ml_dtypes
    bf = ml_dtypes.bfloat16

    x = np.ascontiguousarray(inputs["L0_attn_x"], f32)
    g = lambda n: np.asarray(inputs["L0_attn_" + n], f32)
    mix = np.stack([g("x_r").ravel(), g("x_w").ravel(), g("x_k").ravel(),
                    g("x_v").ravel(), g("x_a").ravel(), g("x_g").ravel()], 0)
    mixv = np.ascontiguousarray(mix.reshape(6, NCK, P).transpose(2, 0, 1).reshape(P, 6 * NCK))

    Wr, Wk, Wv, Wo = g("Wr"), g("Wk"), g("Wv"), g("Wo")
    w1, w2 = g("w1"), g("w2")
    a1, a2 = g("a1"), g("a2")
    g1, g2_ = g("g1"), g("g2")
    l1 = np.concatenate([w1, a1, g1], 1)

    ident = np.eye(P, dtype=bf)
    ones1 = np.ones((P, P), dtype=bf)
    onesS = np.full((P, P), 1.0 / C, dtype=np.float32).astype(bf)
    blk64 = np.kron(np.eye(2, dtype=np.float32), np.ones((N, N), np.float32))
    blockS = (blk64 / N).astype(bf)
    blockU = blk64.astype(bf)
    mus64 = np.triu(np.ones((N, N), np.float32), 1)
    mui64 = np.triu(np.ones((N, N), np.float32))
    mcomb_h = np.concatenate([mus64, mui64], 1)
    m_comb = np.concatenate([mcomb_h, mcomb_h], 0).astype(bf)
    mls64 = np.tril(np.ones((N, N), np.float32), -1)
    m_ls = np.concatenate([mls64, mls64], 0).astype(bf)
    i64 = np.concatenate([np.eye(N, dtype=np.float32)] * 2, 0).astype(bf)
    selm_h = np.zeros((P, 512), np.float32)
    for c in range(4):
        selm_h[c, c * 128:(c + 1) * 128] = 1.0
    selm_h = selm_h.astype(bf)

    def chpack(vecs):
        out = np.zeros((P, 8 * SCK), f32)
        for vi, v in enumerate(vecs):
            out[:, vi * SCK:(vi + 1) * SCK] = v.reshape(SCK, P).T
        return out

    in_maps = []
    for core in range(8):
        b, hg = core // 2, core % 2
        chs = slice(CH * hg, CH * (hg + 1))
        chc = chpack([
            g("k_k").ravel()[chs], g("k_a").ravel()[chs], g("w0").ravel()[chs],
            g("a0").ravel()[chs], g("ln_gamma").ravel()[chs], g("ln_beta").ravel()[chs],
            g("r_k").reshape(16, N)[8 * hg:8 * hg + 8].ravel(),
            np.zeros(CH, f32),
        ])
        im = {
            "x_seq": np.ascontiguousarray(x[b]),
            "x_res": np.ascontiguousarray(x[b, OT * hg:OT * (hg + 1)]),
            "segbase": np.array([[hg]], np.int32),
            "wr": np.ascontiguousarray(Wr[:, chs]),
            "wk": np.ascontiguousarray(Wk[:, chs]),
            "wv": np.ascontiguousarray(Wv[:, chs]),
            "wo": np.ascontiguousarray(Wo),
            "l1": np.ascontiguousarray(l1),
            "w2a2": np.ascontiguousarray(np.concatenate([w2[:, chs], a2[:, chs]], 1)),
            "g2": np.ascontiguousarray(g2_[:, chs]),
            "mixv": mixv,
            "chconst": chc,
            "wgate": np.ascontiguousarray(np.asarray(inputs["L0_ffn_w_gate"], f32)),
            "wup": np.ascontiguousarray(np.asarray(inputs["L0_ffn_w_up"], f32)),
            "wdown": np.ascontiguousarray(np.asarray(inputs["L0_ffn_w_down"], f32)),
            "identc": ident, "ones1": ones1, "onesS": onesS,
            "selm": selm_h,
            "blockS": blockS, "blockU": blockU,
            "m_comb": m_comb, "m_ls64": m_ls, "i64": i64,
        }
        in_maps.append(im)

    res = run_bass_kernel_spmd(nc, in_maps, core_ids=list(range(8)))
    out = np.empty((4, T, C), np.float32)
    for core in range(8):
        b, hg = core // 2, core % 2
        out[b, OT * hg:OT * (hg + 1)] = np.ascontiguousarray(res.results[core]["out_fm"]).T
    return out
